# version 04
# baseline (speedup 1.0000x reference)
import os
import numpy as np
import ml_dtypes
from contextlib import ExitStack
NPHASES = int(os.environ.get("KERNEL_NPHASES", "5"))
GCHUNKS = int(os.environ.get("KERNEL_GCHUNKS", "8"))
QOTS = int(os.environ.get("KERNEL_QOTS", "16"))
NOROPE = bool(os.environ.get("KERNEL_NOROPE"))
ROPEVAR = int(os.environ.get("KERNEL_ROPEVAR", "0"))

import concourse.bass as bass
import concourse.mybir as mybir
from concourse.bass_utils import run_bass_kernel_spmd

F32 = mybir.dt.float32
F16 = mybir.dt.float16
BF16 = mybir.dt.bfloat16
AF = mybir.ActivationFunctionType
ALU = mybir.AluOpType
AX = mybir.AxisListType

P = 128
D = 2048
NTOK = 8192
H = 16
FFN = 8192
LOCAL = 1024
CORES = 8
CC = 16
JT = FFN // P
LNEPS = 1e-5
PRE = 1.0 / 32.0
EPSDEN = 1e-6 * (PRE ** 4)


class Ctr:
    def __init__(self):
        self.v = 0

    def inc(self, n=1):
        self.v += n
        return self.v


class Phase:

    def __init__(self, nc, name):
        self.nc = nc
        self.name = name
        self.ops = {k: [] for k in ("sync", "gpsimd", "tensor", "scalar", "vector")}

    def __call__(self, eng, fn):
        self.ops[eng].append(fn)

    def run(self):
        with self.nc.Block(self.name) as blk:
            for eng in ("sync", "gpsimd", "tensor", "scalar", "vector"):
                if not self.ops[eng]:
                    continue
                runner = (lambda ops: (lambda e: [fn(e) for fn in ops]))(self.ops[eng])
                getattr(blk, eng)(runner)


def build_program():
    nc = bass.Bass("TRN2", target_bir_lowering=False)

    x_all = nc.dram_tensor("x_all", [NTOK, D], F32, kind="ExternalInput").ap()
    x_mine = nc.dram_tensor("x_mine", [LOCAL, D], F32, kind="ExternalInput").ap()
    wq_t = nc.dram_tensor("wq_t", [H, D, P], F32, kind="ExternalInput").ap()
    wk_t = nc.dram_tensor("wk_t", [H, D, P], F32, kind="ExternalInput").ap()
    wv_m = nc.dram_tensor("wv_m", [CC, P, D], F32, kind="ExternalInput").ap()
    wo_m = nc.dram_tensor("wo_m", [CC, P, D], F32, kind="ExternalInput").ap()
    w0_t = nc.dram_tensor("w0_t", [JT, D, P], F32, kind="ExternalInput").ap()
    w1_t = nc.dram_tensor("w1_t", [CC, FFN, P], F32, kind="ExternalInput").ap()
    b0_t = nc.dram_tensor("b0_t", [P, JT], F32, kind="ExternalInput").ap()
    b1_t = nc.dram_tensor("b1_t", [P, CC], F32, kind="ExternalInput").ap()
    cosk = nc.dram_tensor("cosk", [P, NTOK], F32, kind="ExternalInput").ap()
    sink = nc.dram_tensor("sink", [P, NTOK], F32, kind="ExternalInput").ap()
    cosq = nc.dram_tensor("cosq", [P, LOCAL], F32, kind="ExternalInput").ap()
    sinq = nc.dram_tensor("sinq", [P, LOCAL], F32, kind="ExternalInput").ap()
    masks = nc.dram_tensor("masks", [P, 8 * P], F16, kind="ExternalInput").ap()
    ident = nc.dram_tensor("ident", [P, P], F16, kind="ExternalInput").ap()
    out = nc.dram_tensor("out", [LOCAL, D], F32, kind="ExternalOutput").ap()

    xn1t_hbm = nc.dram_tensor("xn1t_hbm", [CC, P, NTOK], F16, kind="Internal").ap()
    kt_hbm = nc.dram_tensor("kt_hbm", [H, P, NTOK], F16, kind="Internal").ap()
    v_hbm = nc.dram_tensor("v_hbm", [H, 64, P, 129], F16, kind="Internal").ap()
    at_hbm = nc.dram_tensor("at_hbm", [P, CC * LOCAL], F16, kind="Internal").ap()

    root = ExitStack()
    with root:
        def rsb(name, shape, dt):
            return root.enter_context(nc.sbuf_tensor(name, shape, dt))

        def rsem(name):
            return root.enter_context(nc.semaphore(name))

        sid = rsb("sid", [P, P], F16)
        b0_sb = rsb("b0_sb", [P, JT], F32)
        b1_sb = rsb("b1_sb", [P, CC], F32)
        stat = rsb("stat", [P, 24], F32)

        def S(i):
            return stat[:, i:i + 1]

        SUM, MEAN, SSQ, MSQ, VARR, RSTD, MB, DEN, REC = range(9)

        def emit_ln(em, src_ap, dst_ap, scr_ap, v2s, s2v, n, base=0):
            SU, ME, SQ, MQ, VA, RS, MBS = (base + k for k in range(7))

            def v_stats(v, src=src_ap):
                v.reduce_sum(S(SU), src, axis=AX.X)
                v.drain()
                v.tensor_scalar_mul(S(ME), S(SU), 1.0 / n)
                v.drain()
                v.tensor_mul(S(MQ), S(ME), S(ME))

            em("vector", v_stats)

            def s_ssq(s, src=src_ap, scr=scr_ap):
                s.activation(scr, src, AF.Square, accum_out=S(SQ)).then_inc(s2v.sem, 1)

            s2v.c.inc()
            em("scalar", s_ssq)

            def v_var(v, w=s2v.c.v):
                v.wait_ge(s2v.sem, w)
                v.tensor_scalar(S(VA), S(SQ), 1.0 / n, LNEPS, ALU.mult, ALU.add)
                v.drain()
                v.tensor_sub(S(VA), S(VA), S(MQ)).then_inc(v2s.sem, 1)

            v2s.c.inc()
            em("vector", v_var)

            def s_rstd(s, w=v2s.c.v):
                s.wait_ge(v2s.sem, w)
                s.activation(S(RS), S(VA), AF.Ln)
                s.drain()
                s.activation(S(RS), S(RS), AF.Exp, scale=-0.5).then_inc(s2v.sem, 1)

            s2v.c.inc()
            em("scalar", s_rstd)

            def v_mb(v, w=s2v.c.v):
                v.wait_ge(s2v.sem, w)
                v.tensor_mul(S(MBS), S(ME), S(RS))
                v.drain()
                v.tensor_scalar_mul(S(MBS), S(MBS), -1.0).then_inc(v2s.sem, 1)

            v2s.c.inc()
            em("vector", v_mb)

            def s_apply(s, src=src_ap, dst=dst_ap, w=v2s.c.v):
                s.wait_ge(v2s.sem, w)
                s.activation(dst, src, AF.Identity, bias=S(MBS), scale=S(RS)).then_inc(s2v.sem, 1)

            s2v.c.inc()
            em("scalar", s_apply)

        class SemC:
            def __init__(self, name):
                self.sem = rsem(name)
                self.c = Ctr()

        with ExitStack() as sq:
            qT = sq.enter_context(nc.sbuf_tensor("qT", [P, H * LOCAL], F16))

            with ExitStack() as pc:
                psb = lambda n_, s_, d_: pc.enter_context(nc.sbuf_tensor(n_, s_, d_))
                pps = lambda n_, s_, d_: pc.enter_context(nc.psum_tensor(n_, s_, d_))
                em = Phase(nc, "p01")
                ldst = SemC("a_ldst")
                ldx = [SemC("a_ldx0"), SemC("a_ldx1")]
                ldcs = SemC("a_ldcs")
                wld = SemC("a_wld")
                v2s = SemC("a_v2s")
                s2v = SemC("a_s2v")
                mm = SemC("a_mm")
                ev = SemC("a_ev")
                st = SemC("a_st")
                stk = SemC("a_stk")

                wbuf = psb("wbuf", [P, H * D], F16)
                xbuf = [psb(f"xb{i}", [P, D], F32) for i in range(2)]
                xnb = [psb(f"xnb{i}", [P, D], F16) for i in range(2)]
                sscr = psb("sscr", [P, D], F16)
                xn1T = psb("xn1T", [P, CC * 1024], F16)
                ktsb = psb("ktsb", [P, H * 1024], F16)
                cosb = psb("cosb", [P, 1024], F32)
                sinb = psb("sinb", [P, 1024], F32)
                rsw = psb("rsw", [P, 512], F32)
                rt1 = psb("rt1", [P, 512], F32)
                rt2 = psb("rt2", [P, 512], F32)
                tr_ps = [pps(f"trp{i}", [P, 512], F16) for i in range(2)]
                k_ps = [pps(f"kp{i}", [P, 512], F32) for i in range(2)]

                def static_loads(sync):
                    sync.dma_start(out=sid[:], in_=ident[:]).then_inc(ldst.sem, 16)
                    sync.dma_start(out=b0_sb[:], in_=b0_t[:]).then_inc(ldst.sem, 16)
                    sync.dma_start(out=b1_sb[:], in_=b1_t[:]).then_inc(ldst.sem, 16)
                    sync.dma_start(out=cosb[:], in_=cosq[:]).then_inc(ldst.sem, 16)
                    sync.dma_start(out=sinb[:], in_=sinq[:]).then_inc(ldst.sem, 16)

                ldst.c.inc(80)
                em("sync", static_loads)
                STATIC_LD = ldst.c.v

                def ld_wq(g):
                    for ot in range(H):
                        g.dma_start(
                            out=wbuf[:, ot * D:(ot + 1) * D].rearrange("p (ci o) -> p ci o", ci=CC),
                            in_=wq_t[ot].rearrange("(ci p) o -> p ci o", p=P),
                        ).then_inc(wld.sem, 16)

                wld.c.inc(H * 16)
                em("gpsimd", ld_wq)
                WQ_RDY = wld.c.v

                xfree = [0, 0]
                xnfree = [0, 0]
                ev_free_tr = [0, 0]

                def emit_tile(src_dram_row, tt8, xn_gate=None):
                    par = tt8 % 2

                    def ld_x(sync, src=src_dram_row, p=par, w=xfree[par]):
                        if w:
                            sync.wait_ge(s2v.sem, w)
                        sync.dma_start(out=xbuf[p][:], in_=src).then_inc(ldx[p].sem, 16)

                    ldx[par].c.inc(16)
                    em("sync", ld_x)
                    ldw = ldx[par].c.v

                    def w_v(v, w=ldw, p=par):
                        v.wait_ge(ldx[p].sem, w)

                    em("vector", w_v)

                    def w_s(s, w=ldw, p=par, wtr=xnfree[par], xg=xn_gate):
                        s.wait_ge(ldx[p].sem, w)
                        if wtr:
                            s.wait_ge(mm.sem, wtr)
                        if xg:
                            if xg[0]:
                                s.wait_ge(st.sem, xg[0])
                            s.wait_ge(mm.sem, xg[1])

                    em("scalar", w_s)
                    emit_ln(em, xbuf[par][:], xnb[par][:], sscr[:], v2s, s2v, D,
                            base=8 * par)
                    xfree[par] = s2v.c.v
                    APPLIED = s2v.c.v

                    for grp in range(4):
                        def t_tr(t, w=APPLIED, pe=ev_free_tr[grp % 2], g_=grp, xp=par,
                                 wst=STATIC_LD):
                            t.wait_ge(ldst.sem, wst)
                            t.wait_ge(s2v.sem, w)
                            if pe > 0:
                                t.wait_ge(ev.sem, pe)
                            for ii in range(4):
                                i = g_ * 4 + ii
                                ins = t.transpose(tr_ps[g_ % 2][:, ii * 128:(ii + 1) * 128],
                                                  xnb[xp][:, i * 128:(i + 1) * 128], sid[:])
                            ins.then_inc(mm.sem, 1)

                        mm.c.inc()
                        em("tensor", t_tr)

                        def s_ev(s, w=mm.c.v, g_=grp, t8=tt8):
                            s.wait_ge(mm.sem, w)
                            dst = xn1T[:, :].rearrange("p (i c) -> p i c", i=CC)[
                                :, g_ * 4:g_ * 4 + 4, t8 * 128:(t8 + 1) * 128]
                            s.copy(dst, tr_ps[g_ % 2][:, :].rearrange(
                                "p (i c) -> p i c", i=4)).then_inc(ev.sem, 1)

                        ev.c.inc()
                        em("scalar", s_ev)
                        ev_free_tr[grp % 2] = ev.c.v
                    xnfree[par] = mm.c.v
                    return ev.c.v

                def emit_rope(ps_ap, dst_ap, cos_ap, sin_ap, mm_w):
                    if NOROPE:
                        def s_plain(s, pa=ps_ap, dst=dst_ap, w=mm_w):
                            s.wait_ge(mm.sem, w)
                            s.copy(dst, pa).then_inc(ev.sem, 1)
                        ev.c.inc()
                        em("scalar", s_plain)
                        rope_prev_ev[0] = ev.c.v
                        return ev.c.v
                    if ROPEVAR == 1:
                        def s_swap1(s, pa=ps_ap, dst=dst_ap, w=mm_w, wprev=rope_prev_ev[0]):
                            s.wait_ge(mm.sem, w)
                            if wprev:
                                s.wait_ge(ev.sem, wprev)
                            s.copy(rsw[0:64, :], pa[64:128, :])
                            s.drain()
                            s.copy(dst, rsw[:]).then_inc(ev.sem, 1)
                        ev.c.inc()
                        em("scalar", s_swap1)
                        rope_prev_ev[0] = ev.c.v
                        return ev.c.v
                    if ROPEVAR in (4, 5):
                        def v_r45(v, pa=ps_ap, dst=dst_ap, ca=cos_ap, sa=sin_ap,
                                  w=mm_w, rv=ROPEVAR):
                            v.wait_ge(mm.sem, w)
                            v.drain()
                            v.tensor_mul(rt1[:], pa, ca)
                            v.drain()
                            v.tensor_mul(rt2[:], rsw[:], sa)
                            v.drain()
                            if rv == 4:
                                v.tensor_add(rt1[:], rt1[:], rt2[:]).then_inc(ev.sem, 1)
                            else:
                                v.tensor_add(dst, rt1[:], rt2[:]).then_inc(ev.sem, 1)
                        ev.c.inc()
                        em("vector", v_r45)
                        rope_prev_ev[0] = ev.c.v
                        return ev.c.v
                    if ROPEVAR == 3:
                        def s_swap3(s, pa=ps_ap, dst=dst_ap, w=mm_w, wprev=rope_prev_ev[0]):
                            s.wait_ge(mm.sem, w)
                            if wprev:
                                s.wait_ge(ev.sem, wprev)
                            s.copy(rsw[0:64, :], pa[64:128, :])
                            s.copy(rsw[64:128, :], pa[0:64, :])
                            s.drain()
                            s.copy(dst, rsw[:]).then_inc(ev.sem, 1)
                        ev.c.inc()
                        em("scalar", s_swap3)
                        rope_prev_ev[0] = ev.c.v
                        return ev.c.v
                    if ROPEVAR == 2:
                        def v_r2(v, pa=ps_ap, dst=dst_ap, ca=cos_ap, w=mm_w):
                            v.wait_ge(mm.sem, w)
                            v.drain()
                            v.tensor_mul(rt1[:], pa, ca)
                            v.drain()
                            v.tensor_copy(dst, rt1[:]).then_inc(ev.sem, 1)
                        ev.c.inc()
                        em("vector", v_r2)
                        rope_prev_ev[0] = ev.c.v
                        return ev.c.v

                    def s_swap(s, pa=ps_ap, w=mm_w, wprev=rope_prev_ev[0]):
                        s.wait_ge(mm.sem, w)
                        if wprev:
                            s.wait_ge(ev.sem, wprev)
                        s.copy(rsw[0:64, :], pa[64:128, :])
                        s.copy(rsw[64:128, :], pa[0:64, :])
                        s.drain()
                        s.mul(stat[:, 23:24], stat[:, 23:24], 1.0).then_inc(s2v.sem, 1)

                    s2v.c.inc()
                    em("scalar", s_swap)

                    def v_rope(v, pa=ps_ap, dst=dst_ap, ca=cos_ap, sa=sin_ap,
                               wmm=mm_w, wsw=s2v.c.v):
                        v.wait_ge(s2v.sem, wsw)
                        v.drain()
                        v.tensor_mul(rt1[:], pa, ca)
                        v.tensor_mul(rt2[:], rsw[:], sa)
                        v.drain()
                        v.tensor_add(dst, rt1[:], rt2[:]).then_inc(ev.sem, 1)

                    ev.c.inc()
                    em("vector", v_rope)
                    rope_prev_ev[0] = ev.c.v
                    return ev.c.v

                rope_prev_ev = [0]

                def w_static_v(v, w=STATIC_LD):
                    v.wait_ge(ldst.sem, w)

                em("vector", w_static_v)

                for tt8 in range(8):
                    emit_tile(x_mine[tt8 * P:(tt8 + 1) * P, :], tt8)
                XNQ_DONE = ev.c.v

                rope_free = [0, 0]
                for ot in range(QOTS):
                    for half in range(2):
                        par = (ot * 2 + half) % 2

                        def t_q(t, o=ot, hf=half, p=par, wq=WQ_RDY, wx=XNQ_DONE,
                                pf=rope_free[par]):
                            t.wait_ge(wld.sem, wq)
                            t.wait_ge(ev.sem, max(wx, pf))
                            for ci in range(CC):
                                ins = t.matmul(k_ps[p][:],
                                               wbuf[:, o * D + ci * P:o * D + (ci + 1) * P],
                                               xn1T[:, ci * 1024 + hf * 512:ci * 1024 + hf * 512 + 512],
                                               start=(ci == 0), stop=(ci == CC - 1))
                            ins.then_inc(mm.sem, 1)

                        mm.c.inc()
                        em("tensor", t_q)
                        col = ot * 1024 + half * 512
                        rope_free[par] = emit_rope(
                            k_ps[par][:], qT[:, col:col + 512],
                            cosb[:, half * 512:half * 512 + 512],
                            sinb[:, half * 512:half * 512 + 512], mm.c.v)
                Q_ALL_EV = ev.c.v

                def ld_wk(g, w=Q_ALL_EV):
                    g.wait_ge(ev.sem, w)
                    for ot in range(H):
                        g.dma_start(
                            out=wbuf[:, ot * D:(ot + 1) * D].rearrange("p (ci o) -> p ci o", ci=CC),
                            in_=wk_t[ot].rearrange("(ci p) o -> p ci o", p=P),
                        ).then_inc(wld.sem, 16)

                wld.c.inc(H * 16)
                em("gpsimd", ld_wk)
                WK_RDY = wld.c.v

                xn1t_store_at = 0
                ktsb_store_at = 0
                kmm_done = mm.c.v
                for g in range(GCHUNKS):
                    def ld_cs(sync, g_=g, w=ev.c.v):
                        sync.wait_ge(ev.sem, w)
                        sync.dma_start(out=cosb[:], in_=cosk[:, g_ * 1024:(g_ + 1) * 1024]).then_inc(ldcs.sem, 16)
                        sync.dma_start(out=sinb[:], in_=sink[:, g_ * 1024:(g_ + 1) * 1024]).then_inc(ldcs.sem, 16)

                    ldcs.c.inc(32)
                    em("sync", ld_cs)
                    CS_RDY = ldcs.c.v

                    for tt8 in range(8):
                        gate = (xn1t_store_at, kmm_done) if tt8 == 0 else None
                        row0 = g * 1024 + tt8 * P
                        emit_tile(x_all[row0:row0 + P, :], tt8, xn_gate=gate)
                    XN_DONE = ev.c.v

                    def st_xn(sync, g_=g, w=XN_DONE):
                        sync.wait_ge(ev.sem, w)
                        for i in range(CC):
                            sync.dma_start(out=xn1t_hbm[i][:, g_ * 1024:(g_ + 1) * 1024],
                                           in_=xn1T[:, i * 1024:(i + 1) * 1024]).then_inc(st.sem, 16)

                    st.c.inc(CC * 16)
                    em("sync", st_xn)
                    xn1t_store_at = st.c.v

                    rope_done_for_ot = {}
                    for ot in range(H):
                        for half in range(2):
                            par = (ot * 2 + half) % 2

                            def t_k(t, o=ot, hf=half, p=par, wk=WK_RDY, wx=XN_DONE,
                                    pf=rope_free[par], wcs=CS_RDY):
                                t.wait_ge(wld.sem, wk)
                                t.wait_ge(ev.sem, max(wx, pf))
                                for ci in range(CC):
                                    ins = t.matmul(k_ps[p][:],
                                                   wbuf[:, o * D + ci * P:o * D + (ci + 1) * P],
                                                   xn1T[:, ci * 1024 + hf * 512:ci * 1024 + hf * 512 + 512],
                                                   start=(ci == 0), stop=(ci == CC - 1))
                                ins.then_inc(mm.sem, 1)

                            mm.c.inc()
                            em("tensor", t_k)
                            def w_cs(v, w=CS_RDY):
                                v.wait_ge(ldcs.sem, w)

                            em("vector", w_cs)
                            if ktsb_store_at:
                                def w_kst(v, w=ktsb_store_at):
                                    v.wait_ge(stk.sem, w)

                                em("vector", w_kst)
                            col = ot * 1024 + half * 512
                            rope_free[par] = emit_rope(
                                k_ps[par][:], ktsb[:, col:col + 512],
                                cosb[:, half * 512:half * 512 + 512],
                                sinb[:, half * 512:half * 512 + 512], mm.c.v)
                        rope_done_for_ot[ot] = ev.c.v
                    kmm_done = mm.c.v

                    def st_kt(sync, g_=g, rd=dict(rope_done_for_ot)):
                        for o in range(H):
                            sync.wait_ge(ev.sem, rd[o])
                            sync.dma_start(out=kt_hbm[o][:, g_ * 1024:(g_ + 1) * 1024],
                                           in_=ktsb[:, o * 1024:(o + 1) * 1024]).then_inc(stk.sem, 16)

                    stk.c.inc(H * 16)
                    em("sync", st_kt)
                    ktsb_store_at = stk.c.v

                def fin(sync, w=st.c.v, wk=stk.c.v):
                    sync.wait_ge(st.sem, w)
                    sync.wait_ge(stk.sem, wk)

                em("sync", fin)
                em.run()
            if NPHASES < 2:
                return nc

            with ExitStack() as pc:
                psb = lambda n_, s_, d_: pc.enter_context(nc.sbuf_tensor(n_, s_, d_))
                pps = lambda n_, s_, d_: pc.enter_context(nc.psum_tensor(n_, s_, d_))
                em = Phase(nc, "p2")
                ld = SemC("b_ld")
                wld = SemC("b_wld")
                mm = SemC("b_mm")
                ev = SemC("b_ev")
                st = SemC("b_st")

                wvbuf = psb("wvbuf", [P, CC * D], F16)
                xn1T = psb("xn1Tv", [P, CC * 1024], F16)
                vsb = psb("vsb", [P, 8 * H * 129], F16)
                v_ps = [pps(f"vps{i}", [P, 512], F32) for i in range(4)]

                def ld_wv(g):
                    for ci in range(CC):
                        g.dma_start(
                            out=wvbuf[:, ci * D:(ci + 1) * D],
                            in_=wv_m[ci],
                        ).then_inc(wld.sem, 16)

                wld.c.inc(CC * 16)
                em("gpsimd", ld_wv)
                WV_RDY = wld.c.v

                def v_ones(v):
                    v.memset(vsb[:, :].rearrange("p (t h c) -> p t h c", t=8, h=H)[:, :, :, 128:129], 1.0).then_inc(ev.sem, 1)

                ev.c.inc()
                em("vector", v_ones)
                ONES = ev.c.v

                vst_at = 0
                for g in range(8):
                    def ld_xn(sync, g_=g, w=mm.c.v):
                        if w:
                            sync.wait_ge(mm.sem, w)
                        for i in range(CC):
                            sync.dma_start(out=xn1T[:, i * 1024:(i + 1) * 1024],
                                           in_=xn1t_hbm[i][:, g_ * 1024:(g_ + 1) * 1024]).then_inc(ld.sem, 16)

                    ld.c.inc(CC * 16)
                    em("sync", ld_xn)
                    XN_RDY = ld.c.v

                    for tt8 in range(8):
                        def t_v(t, g_=g, t8=tt8, wx=XN_RDY, wv=WV_RDY,
                                we=ev.c.v if (g or tt8) else ONES):
                            t.wait_ge(ld.sem, wx)
                            t.wait_ge(wld.sem, wv)
                            t.wait_ge(ev.sem, we)
                            for ci in range(CC):
                                lhs = xn1T[:, ci * 1024 + t8 * 128:ci * 1024 + (t8 + 1) * 128]
                                for q4 in range(4):
                                    ins = t.matmul(v_ps[q4][:], lhs,
                                                   wvbuf[:, ci * D + q4 * 512:ci * D + (q4 + 1) * 512],
                                                   start=(ci == 0), stop=(ci == CC - 1))
                            ins.then_inc(mm.sem, 1)

                        mm.c.inc()
                        em("tensor", t_v)

                        def s_ev(s, t8=tt8, w=mm.c.v, wst=vst_at):
                            s.wait_ge(mm.sem, w)
                            if wst:
                                s.wait_ge(st.sem, wst)
                            base = t8 * (H * 129)
                            for q4 in range(4):
                                dst = vsb[:, base + q4 * 4 * 129: base + (q4 + 1) * 4 * 129].rearrange(
                                    "p (h c) -> p h c", h=4)[:, :, 0:128]
                                s.copy(dst, v_ps[q4][:, :].rearrange("p (h c) -> p h c", h=4)).then_inc(ev.sem, 1)

                        ev.c.inc(4)
                        em("scalar", s_ev)
                    CH_DONE = ev.c.v

                    def st_v(sync, g_=g, w=CH_DONE):
                        sync.wait_ge(ev.sem, w)
                        for o in range(H):
                            src = vsb[:, :].rearrange("p (t hc) -> p t hc", t=8)[
                                :, :, o * 129:(o + 1) * 129]
                            dst = v_hbm[o, g_ * 8:(g_ + 1) * 8, :, :].rearrange("t p c -> p t c")
                            sync.dma_start(out=dst, in_=src).then_inc(st.sem, 16)

                    st.c.inc(H * 16)
                    em("sync", st_v)
                    vst_at = st.c.v

                def fin(sync, w=st.c.v):
                    sync.wait_ge(st.sem, w)

                em("sync", fin)
                em.run()
            if NPHASES < 3:
                return nc

            with ExitStack() as pc:
                psb = lambda n_, s_, d_: pc.enter_context(nc.sbuf_tensor(n_, s_, d_))
                pps = lambda n_, s_, d_: pc.enter_context(nc.psum_tensor(n_, s_, d_))
                em = Phase(nc, "p3")
                ldm = SemC("c_ldm")
                ldkv = [SemC("c_ldkv0"), SemC("c_ldkv1")]
                mm = SemC("c_mm")
                sc1 = SemC("c_sc1")
                v2 = SemC("c_v2")
                avd = SemC("c_avd")
                vden = SemC("c_vden")
                scl = SemC("c_scl")
                st = SemC("c_st")

                attnT = psb("attnT", [P, CC * LOCAL], F16)
                ktbuf = [psb(f"ktb{i}", [P, NTOK], F16) for i in range(2)]
                vbuf = [psb(f"vb{i}", [P, 64 * 129], F16) for i in range(2)]
                s2b = [psb(f"s2b{i}", [P, 1024], F16) for i in range(2)]
                s4b = [psb(f"s4b{i}", [P, 1024], BF16) for i in range(2)]
                msk = psb("msk", [P, 8 * P], F16)
                atile = psb("atile", [P, P], F16)
                sc_ps = [pps(f"scps{i}", [P, 1024], F32) for i in range(2)]
                acc_ps = [pps(f"acc{i}", [P, 512], F32) for i in range(3)]
                tr3_ps = pps("tr3", [P, P], F16)

                def ACC(t):
                    return acc_ps[t // 3][:, 129 * (t % 3):129 * (t % 3) + 129]

                def ld_msk(sync):
                    sync.dma_start(out=msk[:], in_=masks[:]).then_inc(ldm.sem, 16)

                ldm.c.inc(16)
                em("sync", ld_msk)

                def w_msk(v, w=ldm.c.v):
                    v.wait_ge(ldm.sem, w)

                em("vector", w_msk)

                atile_free = [0]
                tr3_free = [0]
                kv_free_at = {}
                pass1_at = {}
                pass2_at = {}
                avd_at = {}
                sc1_hist = {}
                evac_done_at = {0 - 1: 0}

                for h in range(H):
                    par = h % 2

                    def ld_kv(sync, h_=h, p=par, wfree=kv_free_at.get(h - 2, 0)):
                        if wfree:
                            sync.wait_ge(avd.sem, wfree)
                        sync.dma_start(out=ktbuf[p][:], in_=kt_hbm[h_]).then_inc(ldkv[p].sem, 16)
                        sync.dma_start(
                            out=vbuf[p][:, :].rearrange("p (t c) -> p t c", t=64),
                            in_=v_hbm[h_].rearrange("t p c -> p t c"),
                        ).then_inc(ldkv[p].sem, 16)

                    ldkv[par].c.inc(32)
                    em("sync", ld_kv)
                    KV_RDY = ldkv[par].c.v

                    for g in range(8):
                        f = (8 - g) * 128
                        for i in range(8):
                            u = 8 * g + i
                            upar = u % 2

                            def t_sc(t, h_=h, g_=g, i_=i, p=upar, f_=f, wkv=KV_RDY,
                                     wf=sc1_hist.get((h, u - 2), 0), kp=par):
                                t.wait_ge(ldkv[kp].sem, wkv)
                                if wf:
                                    t.wait_ge(sc1.sem, wf)
                                uu = 8 * g_ + i_
                                lhs = ktbuf[kp][:, uu * 128:(uu + 1) * 128]
                                q0 = h_ * 1024 + g_ * 128
                                nmm = (f_ + 511) // 512
                                for m in range(nmm):
                                    w = min(512, f_ - m * 512)
                                    ins = t.matmul(sc_ps[p][:, m * 512:m * 512 + w], lhs,
                                                   qT[:, q0 + m * 512:q0 + m * 512 + w],
                                                   start=True, stop=True)
                                ins.then_inc(mm.sem, 1)

                            mm.c.inc()
                            em("tensor", t_sc)

                            def s_p1(s, p=upar, f_=f, w=mm.c.v,
                                     wb=pass2_at.get((h, u - 2), 0)):
                                s.wait_ge(mm.sem, w)
                                if wb:
                                    s.wait_ge(v2.sem, wb)
                                s.activation(s2b[p][:, 0:f_], sc_ps[p][:, 0:f_],
                                             AF.Square, scale=PRE).then_inc(sc1.sem, 1)

                            sc1.c.inc()
                            em("scalar", s_p1)
                            sc1_hist[(h, u)] = sc1.c.v
                            pass1_at[(h, u)] = sc1.c.v

                            def v_p2(v, p=upar, i_=i, f_=f, w=sc1.c.v,
                                     wb=avd_at.get((h, u - 2), 0)):
                                v.wait_ge(sc1.sem, w)
                                if wb:
                                    v.wait_ge(avd.sem, wb)
                                v.tensor_mul(s2b[p][:, 0:128], s2b[p][:, 0:128],
                                             msk[:, i_ * 128:(i_ + 1) * 128])
                                v.drain()
                                v.tensor_mul(s4b[p][:, 0:f_], s2b[p][:, 0:f_],
                                             s2b[p][:, 0:f_]).then_inc(v2.sem, 1)

                            v2.c.inc()
                            em("vector", v_p2)
                            pass2_at[(h, u)] = v2.c.v

                            def t_av(t, g_=g, i_=i, p=upar, kp=par, w=v2.c.v,
                                     wacc=evac_done_at.get(h - 1, 0)):
                                t.wait_ge(v2.sem, w)
                                if wacc and g_ == 0 and i_ == 0:
                                    t.wait_ge(scl.sem, wacc)
                                uu = 8 * g_ + i_
                                mv = vbuf[kp][:, uu * 129:(uu + 1) * 129]
                                for tq in range(g_, 8):
                                    ins = t.matmul(ACC(tq), s4b[p][:, (tq - g_) * 128:(tq - g_ + 1) * 128],
                                                   mv, start=(uu == 0 and tq % 3 == 0),
                                                   stop=(uu == 8 * tq + 7))
                                ins.then_inc(avd.sem, 1)

                            avd.c.inc()
                            em("tensor", t_av)
                            avd_at[(h, u)] = avd.c.v
                    kv_free_at[h] = avd.c.v
                    AV_ALL = avd.c.v

                    for tq in range(8):
                        def v_den(v, t_=tq, w=AV_ALL, ws=scl.c.v if tq else 0):
                            v.wait_ge(avd.sem, w)
                            if ws:
                                v.wait_ge(scl.sem, ws)
                            v.tensor_scalar_max(S(DEN), ACC(t_)[:, 128:129], EPSDEN)
                            v.drain()
                            v.reciprocal(S(REC), S(DEN)).then_inc(vden.sem, 1)

                        vden.c.inc()
                        em("vector", v_den)

                        def s_sc(s, t_=tq, w=vden.c.v, wtr=atile_free[0]):
                            s.wait_ge(vden.sem, w)
                            if wtr:
                                s.wait_ge(mm.sem, wtr)
                            s.activation(atile[:], ACC(t_)[:, 0:128], AF.Identity,
                                         scale=S(REC)).then_inc(scl.sem, 1)

                        scl.c.inc()
                        em("scalar", s_sc)

                        def t_tr(t, w=scl.c.v, wtr=tr3_free[0]):
                            t.wait_ge(scl.sem, w)
                            if wtr:
                                t.wait_ge(scl.sem, wtr)
                            t.transpose(tr3_ps[:], atile[:], sid[:]).then_inc(mm.sem, 1)

                        mm.c.inc()
                        em("tensor", t_tr)
                        atile_free[0] = mm.c.v

                        def s_tr(s, h_=h, t_=tq, w=mm.c.v):
                            s.wait_ge(mm.sem, w)
                            s.copy(attnT[:, h_ * 1024 + t_ * 128:h_ * 1024 + (t_ + 1) * 128],
                                   tr3_ps[:]).then_inc(scl.sem, 1)

                        scl.c.inc()
                        em("scalar", s_tr)
                        tr3_free[0] = scl.c.v
                    evac_done_at[h] = scl.c.v

                def st_at(sync, w=scl.c.v):
                    sync.wait_ge(scl.sem, w)
                    sync.dma_start(out=at_hbm[:], in_=attnT[:]).then_inc(st.sem, 16)
                    sync.wait_ge(st.sem, 16)

                em("sync", st_at)
                em.run()
        if NPHASES < 4:
            return nc

        with ExitStack() as s45:
            r1_sb = s45.enter_context(nc.sbuf_tensor("r1_sb", [P, 8 * D], F16))
            xn2T = s45.enter_context(nc.sbuf_tensor("xn2T", [P, CC * 1024], F16))

            with ExitStack() as pc:
                psb = lambda n_, s_, d_: pc.enter_context(nc.sbuf_tensor(n_, s_, d_))
                pps = lambda n_, s_, d_: pc.enter_context(nc.psum_tensor(n_, s_, d_))
                em = Phase(nc, "p4")
                ldat_s = SemC("d_ldat")
                ldxm_s = SemC("d_ldxm")
                wld = SemC("d_wld")
                v2s = SemC("d_v2s")
                s2v = SemC("d_s2v")
                mm = SemC("d_mm")
                ev = SemC("d_ev")

                attnT = psb("attnT4", [P, CC * LOCAL], F16)
                wobuf = psb("wobuf", [P, CC * D], F16)
                zbuf = psb("zbuf", [P, D], F32)
                lnb = psb("lnb", [P, D], F32)
                xmb = psb("xmb", [P, D], F32)
                sscr4 = psb("sscr4", [P, D], F16)
                xnb4 = psb("xnb4", [P, D], F16)
                z_ps = [pps(f"zps{i}", [P, 512], F32) for i in range(4)]
                tr4_ps = [pps(f"tr4{i}", [P, 512], F16) for i in range(2)]

                def ld_at(sync):
                    sync.dma_start(out=attnT[:], in_=at_hbm[:]).then_inc(ldat_s.sem, 16)

                ldat_s.c.inc(16)
                em("sync", ld_at)
                AT_RDY = ldat_s.c.v

                def ld_wo(g):
                    for ci in range(CC):
                        g.dma_start(out=wobuf[:, ci * D:(ci + 1) * D],
                                    in_=wo_m[ci]).then_inc(wld.sem, 16)

                wld.c.inc(CC * 16)
                em("gpsimd", ld_wo)
                WO_RDY = wld.c.v

                ev_free_tr4 = [0, 0]
                res_done_at = 0
                for tq in range(8):
                    def ld_xm(sync, t_=tq, wr=res_done_at):
                        if wr:
                            sync.wait_ge(v2s.sem, wr)
                        sync.dma_start(out=xmb[:], in_=x_mine[t_ * P:(t_ + 1) * P, :]).then_inc(ldxm_s.sem, 16)

                    ldxm_s.c.inc(16)
                    em("sync", ld_xm)

                    def t_wo(t, t_=tq, wa=AT_RDY, ww=WO_RDY, wev=ev.c.v):
                        t.wait_ge(ldat_s.sem, wa)
                        t.wait_ge(wld.sem, ww)
                        if wev:
                            t.wait_ge(ev.sem, wev)
                        for ci in range(CC):
                            lhs = attnT[:, ci * 1024 + t_ * 128:ci * 1024 + (t_ + 1) * 128]
                            for q4 in range(4):
                                ins = t.matmul(z_ps[q4][:], lhs,
                                               wobuf[:, ci * D + q4 * 512:ci * D + (q4 + 1) * 512],
                                               start=(ci == 0), stop=(ci == CC - 1))
                        ins.then_inc(mm.sem, 1)

                    mm.c.inc()
                    em("tensor", t_wo)

                    def s_z(s, w=mm.c.v):
                        s.wait_ge(mm.sem, w)
                        for q4 in range(4):
                            ins = s.copy(zbuf[:, q4 * 512:(q4 + 1) * 512], z_ps[q4][:])
                        ins.then_inc(ev.sem, 1)

                    ev.c.inc()
                    em("scalar", s_z)
                    Z_RDY = ev.c.v

                    def w_z(v, w=Z_RDY):
                        v.wait_ge(ev.sem, w)

                    em("vector", w_z)

                    def w_zs(s, w=Z_RDY):
                        s.wait_ge(ev.sem, w)

                    em("scalar", w_zs)
                    emit_ln(em, zbuf[:], lnb[:], sscr4[:], v2s, s2v, D, base=0)
                    LN3_AP = s2v.c.v

                    def v_res(v, t_=tq, wl=ldxm_s.c.v, wa=LN3_AP):
                        v.wait_ge(ldxm_s.sem, wl)
                        v.wait_ge(s2v.sem, wa)
                        v.tensor_add(r1_sb[:, t_ * D:(t_ + 1) * D], xmb[:], lnb[:]).then_inc(v2s.sem, 1)

                    v2s.c.inc()
                    em("vector", v_res)
                    res_done_at = v2s.c.v
                    R1_RDY = v2s.c.v

                    def w_r1s(s, w=R1_RDY):
                        s.wait_ge(v2s.sem, w)

                    em("scalar", w_r1s)

                    def w_r1v(v, w=R1_RDY):
                        v.wait_ge(v2s.sem, w)

                    em("vector", w_r1v)
                    emit_ln(em, r1_sb[:, tq * D:(tq + 1) * D], xnb4[:], sscr4[:],
                            v2s, s2v, D, base=8)
                    LN2_AP = s2v.c.v

                    for grp in range(4):
                        def t_tr(t, w=LN2_AP, pe=ev_free_tr4[grp % 2], g_=grp):
                            t.wait_ge(s2v.sem, w)
                            if pe:
                                t.wait_ge(ev.sem, pe)
                            for ii in range(4):
                                i = g_ * 4 + ii
                                ins = t.transpose(tr4_ps[g_ % 2][:, ii * 128:(ii + 1) * 128],
                                                  xnb4[:, i * 128:(i + 1) * 128], sid[:])
                            ins.then_inc(mm.sem, 1)

                        mm.c.inc()
                        em("tensor", t_tr)

                        def s_ev(s, w=mm.c.v, g_=grp, t_=tq):
                            s.wait_ge(mm.sem, w)
                            dst = xn2T[:, :].rearrange("p (i c) -> p i c", i=CC)[
                                :, g_ * 4:g_ * 4 + 4, t_ * 128:(t_ + 1) * 128]
                            s.copy(dst, tr4_ps[g_ % 2][:, :].rearrange("p (i c) -> p i c", i=4)).then_inc(ev.sem, 1)

                        ev.c.inc()
                        em("scalar", s_ev)
                        ev_free_tr4[grp % 2] = ev.c.v
                em.run()
            if NPHASES < 5:
                return nc

            with ExitStack() as pc:
                psb = lambda n_, s_, d_: pc.enter_context(nc.sbuf_tensor(n_, s_, d_))
                pps = lambda n_, s_, d_: pc.enter_context(nc.psum_tensor(n_, s_, d_))
                em = Phase(nc, "p5")
                w0s = [SemC("e_w0a"), SemC("e_w0b")]
                w1s = [SemC("e_w1a"), SemC("e_w1b")]
                mm = SemC("e_mm")
                ev = SemC("e_ev")
                v2s = SemC("e_v2s")
                s2v = SemC("e_s2v")
                st = SemC("e_st")

                gt = psb("gt", [P, JT * 512], F16)
                w0b = [psb(f"w0b{i}", [P, D], F16) for i in range(2)]
                w1b = [psb(f"w1b{i}", [P, JT * P], F16) for i in range(2)]
                et = psb("et", [P, CC * 512], F16)
                e_tok = psb("e_tok", [P, D], F16)
                u32 = psb("u32", [P, D], F32)
                sscr5 = psb("sscr5", [P, D], F16)
                dt_ps = [pps(f"dtp{i}", [P, 512], F32) for i in range(2)]
                et_ps = [pps(f"etp{i}", [P, 512], F32) for i in range(2)]
                tr5_ps = [pps(f"tr5{i}", [P, 512], F16) for i in range(2)]

                n_store = 0
                gelu_at = {}
                et_at = {}
                et_mm_all = 0
                tr_mm_all = 0
                for chunk in range(2):
                    for jt in range(JT):
                        par = jt % 2

                        def ld_w0(g, j=jt, p=par, wfree=gelu_at.get((chunk, jt - 2), 0)):
                            if wfree:
                                g.wait_ge(ev.sem, wfree)
                            g.dma_start(
                                out=w0b[p][:, :].rearrange("p (ci o) -> p ci o", ci=CC),
                                in_=w0_t[j].rearrange("(ci p) o -> p ci o", p=P),
                            ).then_inc(w0s[p].sem, 16)

                        w0s[par].c.inc(16)
                        em("gpsimd", ld_w0)
                        W0_RDY = w0s[par].c.v

                        def t_dt(t, j=jt, p=par, ch=chunk, ww=W0_RDY,
                                 wfree=gelu_at.get((chunk, jt - 2), 0)):
                            t.wait_ge(w0s[p].sem, ww)
                            if wfree:
                                t.wait_ge(ev.sem, wfree)
                            for ci in range(CC):
                                ins = t.matmul(dt_ps[p][:],
                                               w0b[p][:, ci * P:(ci + 1) * P],
                                               xn2T[:, ci * 1024 + ch * 512:ci * 1024 + ch * 512 + 512],
                                               start=(ci == 0), stop=(ci == CC - 1))
                            ins.then_inc(mm.sem, 1)

                        mm.c.inc()
                        em("tensor", t_dt)

                        def s_g(s, j=jt, p=par, w=mm.c.v, wet=et_mm_all if jt == 0 else 0):
                            s.wait_ge(mm.sem, w)
                            if wet:
                                s.wait_ge(mm.sem, wet)
                            s.activation(gt[:, j * 512:(j + 1) * 512], dt_ps[p][:],
                                         AF.Gelu, bias=b0_sb[:, j:j + 1]).then_inc(ev.sem, 1)

                        ev.c.inc()
                        em("scalar", s_g)
                        gelu_at[(chunk, jt)] = ev.c.v
                    GT_DONE = ev.c.v

                    for ot in range(CC):
                        par = ot % 2

                        def ld_w1(g, o=ot, p=par, wfree=et_at.get((chunk, ot - 2), 0)):
                            if wfree:
                                g.wait_ge(ev.sem, wfree)
                            g.dma_start(
                                out=w1b[p][:, :].rearrange("p (cj o) -> p cj o", cj=JT),
                                in_=w1_t[o].rearrange("(cj p) o -> p cj o", p=P),
                            ).then_inc(w1s[p].sem, 16)

                        w1s[par].c.inc(16)
                        em("gpsimd", ld_w1)
                        W1_RDY = w1s[par].c.v

                        def t_et(t, o=ot, p=par, wg=GT_DONE, ww=W1_RDY,
                                 wfree=et_at.get((chunk, ot - 2), 0)):
                            t.wait_ge(ev.sem, wg)
                            t.wait_ge(w1s[p].sem, ww)
                            if wfree:
                                t.wait_ge(ev.sem, wfree)
                            for cj in range(JT):
                                ins = t.matmul(et_ps[p][:],
                                               w1b[p][:, cj * P:(cj + 1) * P],
                                               gt[:, cj * 512:(cj + 1) * 512],
                                               start=(cj == 0), stop=(cj == JT - 1))
                            ins.then_inc(mm.sem, 1)

                        mm.c.inc()
                        em("tensor", t_et)

                        def s_e(s, o=ot, p=par, w=mm.c.v, wtr=tr_mm_all if ot == 0 else 0):
                            s.wait_ge(mm.sem, w)
                            if wtr:
                                s.wait_ge(mm.sem, wtr)
                            s.activation(et[:, o * 512:(o + 1) * 512], et_ps[p][:],
                                         AF.Identity, bias=b1_sb[:, o:o + 1]).then_inc(ev.sem, 1)

                        ev.c.inc()
                        em("scalar", s_e)
                        et_at[(chunk, ot)] = ev.c.v
                    ET_DONE = ev.c.v
                    et_mm_all = mm.c.v

                    ev_free5 = [0, 0]
                    for nt in range(4):
                        tglob = chunk * 4 + nt
                        for grp in range(4):
                            def t_tr(t, g_=grp, n_=nt, w=ET_DONE, pe=ev_free5[grp % 2]):
                                t.wait_ge(ev.sem, max(w, pe))
                                for ii in range(4):
                                    o = g_ * 4 + ii
                                    ins = t.transpose(tr5_ps[g_ % 2][:, ii * 128:(ii + 1) * 128],
                                                      et[:, o * 512 + n_ * 128:o * 512 + (n_ + 1) * 128],
                                                      sid[:])
                                ins.then_inc(mm.sem, 1)

                            mm.c.inc()
                            em("tensor", t_tr)

                            def s_ev(s, w=mm.c.v, g_=grp):
                                s.wait_ge(mm.sem, w)
                                s.copy(e_tok[:, g_ * 512:(g_ + 1) * 512],
                                       tr5_ps[g_ % 2][:, :]).then_inc(ev.sem, 1)

                            ev.c.inc()
                            em("scalar", s_ev)
                            ev_free5[grp % 2] = ev.c.v
                        ETOK = ev.c.v

                        def w_et_v(v, w=ETOK):
                            v.wait_ge(ev.sem, w)

                        em("vector", w_et_v)

                        def w_et_s(s, w=ETOK):
                            s.wait_ge(ev.sem, w)

                        em("scalar", w_et_s)
                        emit_ln(em, e_tok[:], u32[:], sscr5[:], v2s, s2v, D, base=0)
                        LN4_AP = s2v.c.v

                        def v_out(v, tg=tglob, w=LN4_AP):
                            v.wait_ge(s2v.sem, w)
                            v.tensor_add(u32[:], u32[:], r1_sb[:, tg * D:(tg + 1) * D]).then_inc(v2s.sem, 1)

                        v2s.c.inc()
                        em("vector", v_out)

                        def st_out(sync, tg=tglob, w=v2s.c.v):
                            sync.wait_ge(v2s.sem, w)
                            sync.dma_start(out=out[tg * P:(tg + 1) * P, :], in_=u32[:]).then_inc(st.sem, 16)

                        st.c.inc(16)
                        em("sync", st_out)
                        n_store += 1
                        ST_NOW = st.c.v

                        def w_st(s, w=ST_NOW):
                            s.wait_ge(st.sem, w)

                        em("scalar", w_st)
                    tr_mm_all = mm.c.v

                def fin(sync, w=st.c.v):
                    sync.wait_ge(st.sem, w)

                em("sync", fin)
                em.run()

    return nc


def _host_prep(inputs, c):
    x = np.asarray(inputs["L0_attn_x"])[0]
    Wq = np.asarray(inputs["L0_attn_Wq"])
    Wk = np.asarray(inputs["L0_attn_Wk"])
    Wv = np.asarray(inputs["L0_attn_Wv"])
    Wo = np.asarray(inputs["L0_attn_Wo"])
    W0 = np.asarray(inputs["L0_ffn_W0"])
    b0 = np.asarray(inputs["L0_ffn_bias0"])
    W1 = np.asarray(inputs["L0_ffn_W1"])
    b1 = np.asarray(inputs["L0_ffn_bias1"])

    im = {}
    im["x_all"] = np.ascontiguousarray(x)
    im["x_mine"] = np.ascontiguousarray(x[c::8])
    for nm, W in (("wq_t", Wq), ("wk_t", Wk)):
        im[nm] = np.ascontiguousarray(W.reshape(H, P, D).transpose(0, 2, 1))
    for nm, W in (("wv_m", Wv), ("wo_m", Wo)):
        im[nm] = np.ascontiguousarray(W.T.reshape(CC, P, D))
    im["w0_t"] = np.ascontiguousarray(W0.reshape(JT, P, D).transpose(0, 2, 1))
    im["w1_t"] = np.ascontiguousarray(W1.reshape(CC, P, FFN).transpose(0, 2, 1))
    im["b0_t"] = np.ascontiguousarray(b0.reshape(JT, P).T)
    im["b1_t"] = np.ascontiguousarray(b1.reshape(CC, P).T)

    half = 64
    theta = (10000.0 ** (-(np.arange(half, dtype=np.float64) * 2.0) / 128.0))
    posg = np.arange(NTOK, dtype=np.float64)
    angg = posg[None, :] * theta[:, None]
    cosg = np.cos(angg).astype(np.float32)
    sing = np.sin(angg).astype(np.float32)
    im["cosk"] = np.ascontiguousarray(np.concatenate([cosg, cosg], 0))
    im["sink"] = np.ascontiguousarray(np.concatenate([-sing, sing], 0))
    posq = (np.arange(LOCAL, dtype=np.float64) * 8 + c)
    angq = posq[None, :] * theta[:, None]
    cq = np.cos(angq).astype(np.float32)
    sq_ = np.sin(angq).astype(np.float32)
    im["cosq"] = np.ascontiguousarray(np.concatenate([cq, cq], 0))
    im["sinq"] = np.ascontiguousarray(np.concatenate([-sq_, sq_], 0))

    pp = np.arange(P)[:, None]
    qc = np.arange(P)[None, :]
    m = np.zeros((P, 8 * P), dtype=np.float16)
    for dj in range(8):
        m[:, dj * P:(dj + 1) * P] = ((128 * dj + pp) <= (8 * qc + c)).astype(np.float16)
    im["masks"] = m
    im["ident"] = np.eye(P, dtype=np.float16)
    return im


def kernel(**inputs):
    nc = build_program()
    in_maps = [_host_prep(inputs, c) for c in range(CORES)]
    res = run_bass_kernel_spmd(nc, in_maps, core_ids=list(range(CORES)))
    full = np.zeros((1, NTOK, D), dtype=np.float32)
    for c in range(CORES):
        full[0, c::8, :] = np.asarray(res.results[c]["out"])
    return full
